# version 13 (final)
# baseline (speedup 1.0000x reference)
import numpy as np
import ml_dtypes
import concourse.bass as bass
import concourse.mybir as mybir
from concourse.bass_utils import run_bass_kernel_spmd
from contextlib import ExitStack

F32 = mybir.dt.float32
F16 = mybir.dt.float16
BF16 = mybir.dt.bfloat16
AF = mybir.ActivationFunctionType
ALU = mybir.AluOpType

P = 128
D = 4096
R = 8192
COLS = 512
SC = D // P
NT = R // P
OWN = NT
M64 = float(1.5 * 2.0 ** 52)
M32 = float(1.5 * 2.0 ** 23)
GQ = 0.5625
EPS = 1e-5
INV256 = 1.0 / 256.0
QMIN, QMAX = -32768.0, 32767.0
PHASES = 3


def build_nc():
    nc = bass.Bass("TRN2", target_bir_lowering=False, detect_race_conditions=False)
    x_own = nc.dram_tensor("x_own", [R, D], F32, kind="ExternalInput")
    x_res = nc.dram_tensor("x_res", [R, COLS], F32, kind="ExternalInput")
    w1 = nc.dram_tensor("w1", [SC, P, D], F32, kind="ExternalInput")
    w2s = nc.dram_tensor("w2s", [D, COLS], F32, kind="ExternalInput")
    masks = nc.dram_tensor("masks", [D, COLS], F32, kind="ExternalInput")
    b2s = nc.dram_tensor("b2s", [1, COLS], F32, kind="ExternalInput")
    wfs = nc.dram_tensor("wfs", [D, COLS], F32, kind="ExternalInput")
    bfs = nc.dram_tensor("bfs", [1, COLS], F32, kind="ExternalInput")
    identb = nc.dram_tensor("identb", [P, P], BF16, kind="ExternalInput")
    identh = nc.dram_tensor("identh", [P, P], F16, kind="ExternalInput")
    onesk = nc.dram_tensor("onesk", [1, P], F32, kind="ExternalInput")
    out = nc.dram_tensor("out", [R, COLS], F32, kind="ExternalOutput")
    tbd = nc.dram_tensor("tbd", [NT, P, D], BF16)
    qd = nc.dram_tensor("qd", [NT, P, D], F16)

    xA = x_own.ap()
    mA = masks.ap()

    es = ExitStack()
    e = es.enter_context
    with es:
        wqT = e(nc.sbuf_tensor("wqT", [P, SC * COLS], F16))
        sidb = e(nc.sbuf_tensor("sidb", [P, P], BF16))
        sidh = e(nc.sbuf_tensor("sidh", [P, P], F16))
        sones = e(nc.sbuf_tensor("sones", [1, P], F32))
        sb2 = e(nc.sbuf_tensor("sb2", [1, COLS], F32))
        sbf = e(nc.sbuf_tensor("sbf", [1, COLS], F32))
        bqrow = e(nc.sbuf_tensor("bqrow", [1, COLS], F32))
        b2rep = e(nc.sbuf_tensor("b2rep", [P, COLS], F32))
        bqrep = e(nc.sbuf_tensor("bqrep", [P, COLS], F32))
        maskb = e(nc.sbuf_tensor("maskb", [P, 2 * COLS], F32))
        phw = e(nc.sbuf_tensor("phw", [P, 64], F32))
        pTb = [e(nc.psum_tensor("pT1", [P, 1024], BF16)),
               e(nc.psum_tensor("pT2", [P, 1024], BF16))]
        pTq = [e(nc.psum_tensor("pT3", [P, 1024], F16)),
               e(nc.psum_tensor("pT4", [P, 1024], F16))]
        pM = [e(nc.psum_tensor("pM0", [P, 512], F32)),
              e(nc.psum_tensor("pM1", [P, 512], F32))]
        pF = [e(nc.psum_tensor("pF0", [P, 512], F32)),
              e(nc.psum_tensor("pF1", [P, 512], F32))]
        S = {}
        for nm in ["qsync", "qscal", "qgps", "qvec", "sWQ", "sAF", "sCH",
                   "sTW", "sTQw", "sTBE", "sQE", "sAG", "sP1",
                   "sW2C", "sW1C", "sAMM", "sAE", "sBC", "sBCE", "sP2",
                   "sYMM", "sFMM", "sU", "sEP", "sGELU"]:
            S[nm] = e(nc.semaphore(nm))

        def W(eng, nm, v):
            if v > 0:
                eng.wait_ge(S[nm], v)

        Q1X = 16 * OWN
        Q2M = Q1X + 16 * SC
        QS_INIT = 16 * 6
        QS_WF = QS_INIT + 16 * 4
        QS_W2 = QS_WF + 16 * 4
        QS_W1 = QS_W2 + 16 * SC

        p1 = ExitStack()
        with p1:
            e1 = p1.enter_context
            xt = [e1(nc.sbuf_tensor("xt0", [P, D], F32)),
                  e1(nc.sbuf_tensor("xt1", [P, D], F32))]
            st_ = e1(nc.sbuf_tensor("st_", [P, D], F32))
            tbt = [e1(nc.sbuf_tensor("tbt0", [P, D], BF16)),
                   e1(nc.sbuf_tensor("tbt1", [P, D], BF16))]
            qt = [e1(nc.sbuf_tensor("qt0", [P, D], F16)),
                  e1(nc.sbuf_tensor("qt1", [P, D], F16))]
            HW = 2048
            B1 = e1(nc.sbuf_tensor("B1", [P, HW], F32))
            B2 = e1(nc.sbuf_tensor("B2", [P, HW], F32))
            B3 = e1(nc.sbuf_tensor("B3", [P, HW], F32))
            tbTs = e1(nc.sbuf_tensor("tbTs", [P, D], BF16))
            qTs = e1(nc.sbuf_tensor("qTs", [P, D], F16))
            bn6 = e1(nc.sbuf_tensor("bn6", [P, 96], F32))
            mvA = e1(nc.sbuf_tensor("mvA", [P, 2 * OWN], F32))
            WS = 64
            vew = e1(nc.sbuf_tensor("vew", [P, WS], F32))
            rsw = e1(nc.sbuf_tensor("rsw", [P, WS], F32))
            nww = e1(nc.sbuf_tensor("nww", [P, WS], F32))
            mrsw = e1(nc.sbuf_tensor("mrsw", [P, WS], F32))
            aqw = e1(nc.sbuf_tensor("aqw", [P, WS], F32))
            bqw = e1(nc.sbuf_tensor("bqw", [P, WS], F32))

            blk = p1.enter_context(nc.Block())

            @blk.sync
            def _(sync):
                for i in range(OWN):
                    W(sync, "sAF", i - 1)
                    sync.dma_start(xt[i % 2][:], xA[i * P:(i + 1) * P, :]).then_inc(S["qsync"], 16)
                sync.sem_inc(S["sP1"], 1)

            @blk.scalar
            def _(scalar):
                scalar.dma_start(sidb[:], identb.ap()).then_inc(S["qscal"], 16)
                scalar.dma_start(sidh[:], identh.ap()).then_inc(S["qscal"], 16)
                scalar.dma_start(sones[:], onesk.ap()).then_inc(S["qscal"], 16)
                scalar.dma_start(sb2[:], b2s.ap()).then_inc(S["qscal"], 16)
                scalar.dma_start(sbf[:], bfs.ap()).then_inc(S["qscal"], 16)
                scalar.dma_start(sbf[:], bfs.ap()).then_inc(S["qscal"], 16)
                for k in range(4):
                    W(scalar, "sWQ", k)
                    scalar.dma_start(st_[:],
                                     bass.AP(wfs, k * 1024 * COLS,
                                             [[COLS, P], [P * COLS, 8], [1, COLS]])).then_inc(S["qscal"], 16)

            @blk.vector
            def _(vector):
                def rne_chain(src, dst, rows, width, clip):
                    b1 = B1[0:rows, 0:width]
                    b2_ = B2[0:rows, 0:width]
                    b3 = B3[0:rows, 0:width]
                    ph = phw[0:rows, 0:1]
                    if clip:
                        vector.tensor_scalar(b1, src, 256.0, QMIN, op0=ALU.mult, op1=ALU.max)
                        vector.tensor_scalar(b2_, b1, QMAX, None, op0=ALU.min)
                        src = b2_
                    vector.tensor_scalar(b3, src, M64, -M64, op0=ALU.add, op1=ALU.add)
                    vector.tensor_scalar(b1, src, M32, -M32, op0=ALU.add, op1=ALU.add)
                    vector.tensor_tensor(b3, b3, b1, op=ALU.add)
                    vector.tensor_scalar(b1, b1, ph, None, op0=ALU.mult)
                    vector.tensor_tensor(dst, b3, b1, op=ALU.subtract)

                W(vector, "qscal", QS_INIT + 16)
                vector.tensor_scalar(phw[:], st_[:, 0:64], 0.0, 1.0, op0=ALU.mult, op1=ALU.add)
                vector.tensor_scalar(phw[:], phw[:], M64, -M64, op0=ALU.add, op1=ALU.add)
                for k in range(4):
                    W(vector, "qscal", QS_INIT + 16 * (k + 1))
                    for c in range(8):
                        jc = 8 * k + c
                        rne_chain(st_[:, c * COLS:(c + 1) * COLS],
                                  wqT[:, jc * COLS:(jc + 1) * COLS], P, COLS, True)
                    vector.drain().then_inc(S["sWQ"], 1)
                W(vector, "qscal", QS_INIT)
                rne_chain(sbf[0:1, :], bqrow[0:1, :], 1, COLS, True)
                vector.drain().then_inc(S["sWQ"], 1)

                for i in range(OWN):
                    W(vector, "qsync", 16 * (i + 1))
                    for c in range(16):
                        vector.bn_stats(bn6[:, 6 * c:6 * (c + 1)], xt[i % 2][:, 256 * c:256 * (c + 1)])
                    vector.bn_aggr(mvA[:, 2 * i:2 * i + 2], bn6[:])
                    vector.drain()
                    vector.tensor_scalar(vew[:], bass.AP(mvA, 2 * i + 1, [[2 * OWN, P], [0, WS]]),
                                         EPS, None, op0=ALU.add)
                    vector.tensor_scalar(rsw[:], vew[:], -0.5, 1.5, op0=ALU.mult, op1=ALU.add)
                    for _ in range(3):
                        vector.tensor_tensor(nww[:], rsw[:], rsw[:], op=ALU.mult)
                        vector.tensor_tensor(nww[:], nww[:], vew[:], op=ALU.mult)
                        vector.tensor_scalar(nww[:], nww[:], -0.5, 1.5, op0=ALU.mult, op1=ALU.add)
                        vector.tensor_tensor(rsw[:], rsw[:], nww[:], op=ALU.mult)
                    vector.tensor_tensor(mrsw[:], bass.AP(mvA, 2 * i, [[2 * OWN, P], [0, WS]]),
                                         rsw[:], op=ALU.mult)
                    vector.tensor_scalar(aqw[:], rsw[:], 256.0, None, op0=ALU.mult)
                    vector.tensor_scalar(bqw[:], mrsw[:], 256.0, None, op0=ALU.mult)
                    W(vector, "sTW", 4 * (i - 1))
                    vector.tensor_scalar(tbt[i % 2][:], xt[i % 2][:], rsw[:, 0:1], mrsw[:, 0:1],
                                         op0=ALU.mult, op1=ALU.subtract)
                    vector.tensor_scalar(st_[:], xt[i % 2][:], aqw[:, 0:1], bqw[:, 0:1],
                                         op0=ALU.mult, op1=ALU.subtract)
                    vector.drain().then_inc(S["sAF"], 1)
                    W(vector, "sTQw", 4 * (i - 1))
                    for h in range(2):
                        rne_chain(st_[:, h * HW:(h + 1) * HW],
                                  qt[i % 2][:, h * HW:(h + 1) * HW], P, HW, False)
                    vector.drain().then_inc(S["sCH"], 1)
                    for w in range(4):
                        idx = 4 * i + w
                        W(vector, "sTW", idx + 1)
                        if w == 0 and i > 0:
                            W(vector, "qgps", 32 * (i - 1) + 16)
                        vector.tensor_copy(tbTs[:, w * 1024:(w + 1) * 1024], pTb[idx % 2][:])
                        vector.drain().then_inc(S["sTBE"], 1)
                    for w in range(4):
                        idx = 4 * i + w
                        W(vector, "sTQw", idx + 1)
                        if w == 0 and i > 0:
                            W(vector, "qgps", 32 * i)
                        vector.tensor_copy(qTs[:, w * 1024:(w + 1) * 1024], pTq[idx % 2][:])
                        vector.drain().then_inc(S["sQE"], 1)
                vector.sem_inc(S["sP1"], 1)

            @blk.tensor
            def _(tensor):
                for i in range(OWN):
                    W(tensor, "sAF", i + 1)
                    for w in range(4):
                        idx = 4 * i + w
                        W(tensor, "sTBE", idx - 1)
                        for b in range(8):
                            sc = 8 * w + b
                            tt = tensor.transpose(pTb[idx % 2][:, b * P:(b + 1) * P],
                                                  tbt[i % 2][:, sc * P:(sc + 1) * P], sidb[:])
                        tt.then_inc(S["sTW"], 1)
                    W(tensor, "sCH", i + 1)
                    for w in range(4):
                        idx = 4 * i + w
                        W(tensor, "sQE", idx - 1)
                        for b in range(8):
                            sc = 8 * w + b
                            tt = tensor.transpose(pTq[idx % 2][:, b * P:(b + 1) * P],
                                                  qt[i % 2][:, sc * P:(sc + 1) * P], sidh[:])
                        tt.then_inc(S["sTQw"], 1)

            @blk.gpsimd
            def _(gpsimd):
                for i in range(OWN):
                    W(gpsimd, "sTBE", 4 * (i + 1))
                    gpsimd.dma_start(bass.AP(tbd, i * P * D, [[D, P], [1, D]]),
                                     tbTs[:]).then_inc(S["qgps"], 16)
                    W(gpsimd, "sQE", 4 * (i + 1))
                    gpsimd.dma_start(bass.AP(qd, i * P * D, [[D, P], [1, D]]),
                                     qTs[:]).then_inc(S["qgps"], 16)
                    gpsimd.wait_ge(S["qgps"], 32 * (i + 1))
                    gpsimd.sem_inc(S["sAG"], 1)
                gpsimd.sem_inc(S["sP1"], 1)

        if PHASES == 1:
            dbg_tb = nc.dram_tensor("dbg_tb", [NT, P, D], BF16, kind="ExternalOutput")
            dbg_q = nc.dram_tensor("dbg_q", [NT, P, D], F16, kind="ExternalOutput")
            dbg_wq = nc.dram_tensor("dbg_wq", [P, SC * COLS], F16, kind="ExternalOutput")
            dbg_rs = nc.dram_tensor("dbg_rs", [P, 2 * OWN], F32, kind="ExternalOutput")
            with nc.Block() as blkz:
                @blkz.gpsimd
                def _(gpsimd):
                    W(gpsimd, "sP1", 3)
                    gpsimd.dma_start(dbg_tb.ap(), tbd.ap()).then_inc(S["qvec"], 16)
                    gpsimd.dma_start(dbg_q.ap(), qd.ap()).then_inc(S["qvec"], 16)
                    gpsimd.dma_start(dbg_wq.ap(), wqT[:]).then_inc(S["qvec"], 16)
                    gpsimd.dma_start(bass.AP(dbg_rs, 0, [[2 * OWN, P], [1, OWN]]),
                                     bass.AP(mvA, 0, [[2 * OWN, P], [1, OWN]])).then_inc(S["qvec"], 16)
                    gpsimd.dma_start(bass.AP(out, 0, [[COLS, P], [1, COLS]]),
                                     b2rep[:]).then_inc(S["qvec"], 16)
                    gpsimd.wait_ge(S["qvec"], 80)
            return nc

        p23 = ExitStack()
        with p23:
            attn = p23.enter_context(nc.sbuf_tensor("attn", [P, SC * COLS], BF16))

            p2 = ExitStack()
            with p2:
                e2 = p2.enter_context
                w1st = [e2(nc.sbuf_tensor("w1st0", [P, D], F32)),
                        e2(nc.sbuf_tensor("w1st1", [P, D], F32))]
                w1c = [e2(nc.sbuf_tensor("w1c0", [P, D], BF16)),
                       e2(nc.sbuf_tensor("w1c1", [P, D], BF16))]
                w2st = e2(nc.sbuf_tensor("w2st", [P, D], F32))
                w2T = e2(nc.sbuf_tensor("w2T", [P, SC * COLS], BF16))

                blk2 = p2.enter_context(nc.Block())

                @blk2.sync
                def _(sync):
                    W(sync, "sP1", 3)
                    for k in range(SC):
                        W(sync, "sAE", k - 1)
                        sync.dma_start(maskb[:, (k % 2) * COLS:(k % 2 + 1) * COLS],
                                       mA[k * P:(k + 1) * P, :]).then_inc(S["qsync"], 16)
                    sync.sem_inc(S["sP2"], 1)

                @blk2.scalar
                def _(scalar):
                    W(scalar, "sP1", 3)
                    for tc in range(4):
                        W(scalar, "sW2C", tc)
                        scalar.dma_start(w2st[:],
                                         bass.AP(w2s, tc * 1024 * COLS,
                                                 [[COLS, P], [P * COLS, 8], [1, COLS]])).then_inc(S["qscal"], 16)
                    for k in range(SC):
                        W(scalar, "sW1C", k - 1)
                        scalar.dma_start(w1st[k % 2][:],
                                         bass.AP(w1, k * P * D, [[D, P], [1, D]])).then_inc(S["qscal"], 16)

                @blk2.vector
                def _(vector):
                    W(vector, "sP1", 3)
                    for tc in range(4):
                        vector.wait_ge(S["qscal"], QS_WF + 16 * (tc + 1))
                        vector.tensor_copy(w2T[:, tc * 4096:(tc + 1) * 4096], w2st[:])
                        vector.drain().then_inc(S["sW2C"], 1)
                    W(vector, "sBC", 1)
                    vector.tensor_copy(b2rep[:], pF[0][:])
                    W(vector, "sBC", 2)
                    vector.tensor_copy(bqrep[:], pF[1][:])
                    vector.drain().then_inc(S["sBCE"], 1)
                    for k in range(SC):
                        vector.wait_ge(S["qscal"], QS_W2 + 16 * (k + 1))
                        W(vector, "sAMM", k - 1)
                        vector.tensor_copy(w1c[k % 2][:], w1st[k % 2][:])
                        vector.drain().then_inc(S["sW1C"], 1)
                        W(vector, "sAMM", k + 1)
                        vector.wait_ge(S["qsync"], Q1X + 16 * (k + 1))
                        vector.tensor_tensor(attn[:, k * COLS:(k + 1) * COLS], pM[k % 2][:],
                                             maskb[:, (k % 2) * COLS:(k % 2 + 1) * COLS],
                                             op=ALU.mult).then_inc(S["sAE"], 1)
                    vector.drain().then_inc(S["sP2"], 1)

                @blk2.tensor
                def _(tensor):
                    W(tensor, "sP1", 3)
                    tensor.matmul(pF[0][:], sones[0:1, :], sb2[0:1, :],
                                  start=True, stop=True).then_inc(S["sBC"], 1)
                    tensor.matmul(pF[1][:], sones[0:1, :], bqrow[0:1, :],
                                  start=True, stop=True).then_inc(S["sBC"], 1)
                    for k in range(SC):
                        W(tensor, "sW1C", k + 1)
                        W(tensor, "sAE", k - 1)
                        for dc in range(SC):
                            mm = tensor.matmul(pM[k % 2][:], w1c[k % 2][:, dc * P:(dc + 1) * P],
                                               w2T[:, dc * COLS:(dc + 1) * COLS],
                                               start=(dc == 0), stop=(dc == SC - 1))
                        mm.then_inc(S["sAMM"], 1)

            if PHASES == 2:
                dbg_at = nc.dram_tensor("dbg_at", [P, SC * COLS], BF16, kind="ExternalOutput")
                dbg_b2 = nc.dram_tensor("dbg_b2", [P, 2 * COLS], F32, kind="ExternalOutput")
                with nc.Block() as blkz:
                    @blkz.gpsimd
                    def _(gpsimd):
                        W(gpsimd, "sP2", 2)
                        gpsimd.dma_start(dbg_at.ap(), attn[:]).then_inc(S["qvec"], 16)
                        gpsimd.dma_start(bass.AP(dbg_b2, 0, [[2 * COLS, P], [1, COLS]]),
                                         b2rep[:]).then_inc(S["qvec"], 16)
                        gpsimd.dma_start(bass.AP(dbg_b2, COLS, [[2 * COLS, P], [1, COLS]]),
                                         bqrep[:]).then_inc(S["qvec"], 16)
                        gpsimd.dma_start(bass.AP(out, 0, [[COLS, P], [1, COLS]]),
                                         b2rep[:]).then_inc(S["qvec"], 16)
                        gpsimd.wait_ge(S["qvec"], 64)
                return nc

            p3 = ExitStack()
            with p3:
                e3 = p3.enter_context
                tbl = [e3(nc.sbuf_tensor("tbl0", [P, D], BF16)),
                       e3(nc.sbuf_tensor("tbl1", [P, D], BF16))]
                ql = [e3(nc.sbuf_tensor("ql0", [P, D], F16)),
                      e3(nc.sbuf_tensor("ql1", [P, D], F16))]
                xrs = [e3(nc.sbuf_tensor("xrs0", [P, COLS], F32)),
                       e3(nc.sbuf_tensor("xrs1", [P, COLS], F32))]
                usb = [e3(nc.sbuf_tensor("usb0", [P, COLS], F32)),
                       e3(nc.sbuf_tensor("usb1", [P, COLS], F32))]
                gts = [e3(nc.sbuf_tensor("gts0", [P, COLS], F32)),
                       e3(nc.sbuf_tensor("gts1", [P, COLS], F32))]
                osb = [e3(nc.sbuf_tensor("osb0", [P, COLS], F32)),
                       e3(nc.sbuf_tensor("osb1", [P, COLS], F32))]
                E1 = e3(nc.sbuf_tensor("E1", [P, COLS], F32))
                E2 = e3(nc.sbuf_tensor("E2", [P, COLS], F32))
                E3 = e3(nc.sbuf_tensor("E3", [P, COLS], F32))
                E4 = e3(nc.sbuf_tensor("E4", [P, COLS], F32))

                blk3 = p3.enter_context(nc.Block())

                @blk3.sync
                def _(sync):
                    W(sync, "sP2", 2)
                    for g in range(NT):
                        W(sync, "sAG", g + 1)
                        W(sync, "sFMM", g - 1)
                        sync.dma_start(tbl[g % 2][:],
                                       bass.AP(tbd, g * P * D, [[D, P], [1, D]])).then_inc(S["qsync"], 16)
                        sync.dma_start(ql[g % 2][:],
                                       bass.AP(qd, g * P * D, [[D, P], [1, D]])).then_inc(S["qsync"], 16)

                @blk3.scalar
                def _(scalar):
                    W(scalar, "sP2", 2)
                    for idx in range(NT):
                        W(scalar, "sEP", idx - 1)
                        scalar.dma_start(xrs[idx % 2][:],
                                         bass.AP(x_res, idx * P * COLS,
                                                 [[COLS, P], [1, COLS]])).then_inc(S["qscal"], 16)
                        W(scalar, "sU", idx + 1)
                        W(scalar, "sEP", idx - 1)
                        scalar.activation(gts[idx % 2][:], usb[idx % 2][:], AF.Gelu)
                        scalar.mul(gts[idx % 2][:, 0:1], gts[idx % 2][:, 0:1],
                                   1.0).then_inc(S["sGELU"], 1)

                @blk3.tensor
                def _(tensor):
                    W(tensor, "sP2", 2)
                    for idx in range(NT):
                        tensor.wait_ge(S["qsync"], Q2M + 32 * idx + 16)
                        W(tensor, "sU", idx - 1)
                        for sc in range(SC):
                            mm = tensor.matmul(pM[idx % 2][:],
                                               tbl[idx % 2][:, sc * P:(sc + 1) * P],
                                               attn[:, sc * COLS:(sc + 1) * COLS],
                                               start=(sc == 0), stop=(sc == SC - 1))
                        mm.then_inc(S["sYMM"], 1)
                        tensor.wait_ge(S["qsync"], Q2M + 32 * idx + 32)
                        W(tensor, "sEP", idx - 1)
                        for sc in range(SC):
                            mm = tensor.matmul(pF[idx % 2][:],
                                               ql[idx % 2][:, sc * P:(sc + 1) * P],
                                               wqT[:, sc * COLS:(sc + 1) * COLS],
                                               start=(sc == 0), stop=(sc == SC - 1))
                        mm.then_inc(S["sFMM"], 1)

                @blk3.vector
                def _(vector):
                    W(vector, "sP2", 2)
                    W(vector, "sBCE", 1)
                    for idx in range(NT):
                        W(vector, "sYMM", idx + 1)
                        W(vector, "sGELU", idx - 1)
                        vector.scalar_tensor_tensor(usb[idx % 2][:], pM[idx % 2][:], 0.0, b2rep[:],
                                                    op0=ALU.bypass, op1=ALU.add)
                        vector.drain().then_inc(S["sU"], 1)
                        W(vector, "sFMM", idx + 1)
                        vector.tensor_scalar(E1[:], pF[idx % 2][:], INV256, None, op0=ALU.mult)
                        vector.tensor_scalar(E2[:], E1[:], M64, -M64, op0=ALU.add, op1=ALU.add)
                        vector.tensor_scalar(E3[:], E1[:], M32, -M32, op0=ALU.add, op1=ALU.add)
                        vector.tensor_tensor(E2[:], E2[:], E3[:], op=ALU.add)
                        vector.tensor_scalar(E3[:], E3[:], phw[:, 0:1], None, op0=ALU.mult)
                        vector.tensor_tensor(E2[:], E2[:], E3[:], op=ALU.subtract)
                        vector.tensor_tensor(E3[:], E1[:], E2[:], op=ALU.is_lt)
                        vector.tensor_tensor(E1[:], E2[:], E3[:], op=ALU.subtract)
                        vector.tensor_scalar(E2[:], E1[:], QMIN, QMAX, op0=ALU.max, op1=ALU.min)
                        vector.scalar_tensor_tensor(E1[:], E2[:], 0.0, bqrep[:], op0=ALU.bypass, op1=ALU.add)
                        vector.tensor_scalar(E2[:], E1[:], QMIN, QMAX, op0=ALU.max, op1=ALU.min)
                        vector.wait_ge(S["qscal"], QS_W1 + 16 * (idx + 1))
                        vector.scalar_tensor_tensor(E1[:], E2[:], INV256, xrs[idx % 2][:],
                                                    op0=ALU.mult, op1=ALU.add)
                        W(vector, "sGELU", idx + 1)
                        if idx > 1:
                            vector.wait_ge(S["qvec"], 16 * (idx - 1))
                        vector.tensor_tensor(osb[idx % 2][:], E1[:], gts[idx % 2][:], op=ALU.add)
                        vector.drain().then_inc(S["sEP"], 1)
                    vector.wait_ge(S["qvec"], 16 * NT)

                @blk3.gpsimd
                def _(gpsimd):
                    for idx in range(NT):
                        W(gpsimd, "sEP", idx + 1)
                        gpsimd.dma_start(bass.AP(out, idx * P * COLS, [[COLS, P], [1, COLS]]),
                                         osb[idx % 2][:]).then_inc(S["qvec"], 16)
                    gpsimd.wait_ge(S["qvec"], 16 * NT)

    return nc


def kernel(L0_attn_x, L0_attn_w1, L0_attn_w2, L0_attn_b2, L0_attn_sparse_mask,
           L0_ffn_weight, L0_ffn_bias):
    x = np.ascontiguousarray(np.asarray(L0_attn_x, dtype=np.float32).reshape(R, D))
    w1 = np.asarray(L0_attn_w1, dtype=np.float32)
    w1r = np.ascontiguousarray(
        w1.reshape(SC, P, SC, P).transpose(2, 1, 0, 3).reshape(SC, P, D))
    w2 = np.ascontiguousarray(np.asarray(L0_attn_w2, dtype=np.float32))
    b2 = np.asarray(L0_attn_b2, dtype=np.float32)
    mask = np.ascontiguousarray(np.asarray(L0_attn_sparse_mask, dtype=np.float32))
    wf = np.ascontiguousarray(np.asarray(L0_ffn_weight, dtype=np.float32))
    bf = np.asarray(L0_ffn_bias, dtype=np.float32)

    nc = build_nc()
    in_maps = []
    for c in range(8):
        cs = slice(c * COLS, (c + 1) * COLS)
        in_maps.append({
            "x_own": x,
            "x_res": np.ascontiguousarray(x[:, cs]),
            "w1": w1r,
            "w2s": np.ascontiguousarray(w2[cs, :].T),
            "masks": np.ascontiguousarray(mask[:, cs]),
            "b2s": np.ascontiguousarray(b2[cs].reshape(1, COLS)),
            "wfs": np.ascontiguousarray(wf[cs, :].T),
            "bfs": np.ascontiguousarray(bf[cs].reshape(1, COLS)),
            "identb": np.eye(P, dtype=ml_dtypes.bfloat16),
            "identh": np.eye(P, dtype=np.float16),
            "onesk": np.ones((1, P), dtype=np.float32),
        })
    res = run_bass_kernel_spmd(nc, in_maps, core_ids=list(range(8)))
    outs = [np.asarray(r["out"], dtype=np.float32) for r in res.results]
    full = np.concatenate(outs, axis=1)
    return full.reshape(2, 4096, 4096)
